# version 28
# baseline (speedup 1.0000x reference)
import numpy as np
from contextlib import ExitStack

import concourse.bass as bass
import concourse.mybir as mybir
from concourse.bass_utils import run_bass_kernel_spmd

F32 = mybir.dt.float32
BF16 = mybir.dt.bfloat16
ALU = mybir.AluOpType
ACT = mybir.ActivationFunctionType

NCORES = 8
NT = 16
T = 256
NG = T // 128
NCH = 8
NE = 8
D = 1024
EPS = 1e-5
INV_D = 1.0 / D
NXB = 3
NWB = 4


def build_program(nt_count=NT):
    nc = bass.Bass("TRN2", target_bir_lowering=False)

    x_in = nc.dram_tensor("x_in", [NT, 128, NCH, T], F32, kind="ExternalInput")
    x_out = nc.dram_tensor("x_out", [NT, 128, NCH, T], F32, kind="ExternalOutput")
    w_exp = nc.dram_tensor("w_exp", [NE, 128, NCH * D], F32, kind="ExternalInput")
    w_attn = nc.dram_tensor("w_attn", [4, 4, 128, NCH * 256], F32, kind="ExternalInput")
    w_gate = nc.dram_tensor("w_gate", [128, NCH * NE], F32, kind="ExternalInput")
    biases = nc.dram_tensor("biases", [128, 32], F32, kind="ExternalInput")
    c_onescol = nc.dram_tensor("c_onescol", [128, 1], F32, kind="ExternalInput")
    c_onesrow = nc.dram_tensor("c_onesrow", [1, 128], F32, kind="ExternalInput")
    c_id = nc.dram_tensor("c_id", [128, 128], F32, kind="ExternalInput")
    c_oneh = nc.dram_tensor("c_oneh", [8, 1024], F32, kind="ExternalInput")
    c_ind = nc.dram_tensor("c_ind", [128, 128], F32, kind="ExternalInput")
    c_expd = nc.dram_tensor("c_expd", [16, 1024], F32, kind="ExternalInput")
    w_attn_bf = nc.dram_tensor("w_attn_bf", [4, 4, 128, NCH * 256], BF16)

    ctx = ExitStack()
    sb = lambda name, shape, dt: ctx.enter_context(nc.sbuf_tensor(name, shape, dt))
    ps = lambda name, shape, dt: ctx.enter_context(nc.psum_tensor(name, shape, dt))
    sem = lambda name: ctx.enter_context(nc.semaphore(name))

    Wexp = sb("Wexp", [128, NE * NCH * D], BF16)
    Wab = sb("Wab", [128, NWB * NCH * 256], BF16)
    Wg = sb("Wg", [128, NCH * NE], BF16)
    Bias = sb("Bias", [128, 32], F32)
    onescol = sb("onescol", [128, 1], BF16)
    onesrow = sb("onesrow", [1, 128], BF16)
    id128 = sb("id128", [128, 128], BF16)
    oneh = sb("oneh", [8, 1024], BF16)
    IND = sb("IND", [128, 128], BF16)
    EXPD = sb("EXPD", [16, 1024], BF16)
    XT = [sb(f"XT{i}", [128, NCH * T], F32) for i in range(NXB)]

    P = [ps(f"P{i}", [128, 512], F32) for i in range(8)]

    class Stream:
        def __init__(self, sid):
            self.sid = sid
            tag = "AB"[sid]
            self.SA = sb("SA" + tag, [128, NCH * T], BF16)
            self.SC = sb("SC" + tag, [128, NCH * T], BF16)
            self.yec = sb("yec" + tag, [128, 8 * T], BF16)
            self.rbb = sb("rbb" + tag, [128, T], BF16)
            self.mrbb = sb("mrbb" + tag, [128, T], BF16)
            self.Msb = sb("Msb" + tag, [1, T], F32)
            self.Rb = sb("Rb" + tag, [1, T], BF16)
            self.MRb = sb("MRb" + tag, [1, T], BF16)
            self.Lg = sb("Lg" + tag, [128, NG * 8], F32)
            self.M8 = sb("M8" + tag, [128, NG * 8 + NG], F32)
            self.ELg = sb("ELg" + tag, [128, NG * 8 + NG], F32)
            self.SELg = sb("SELg" + tag, [128, NG * 8], F32)
            self.wtm = sb("wtm" + tag, [128, NG * 8], BF16)
            self.D2 = sb("D2" + tag, [128, NG], F32)
            self.RR = sb("RR" + tag, [128, NG], F32)
            self.w_em = sb("wem" + tag, [8, T], BF16)
            self.Cf = sb("Cf" + tag, [16, T], BF16)
            self.B = P[4 * sid:4 * sid + 4]
            self.B3b = P[4 * sid + 3][:, :].bitcast(BF16)
            self.B2b = P[4 * sid + 2][:, :].bitcast(BF16)
            self.st = {}

    S = [Stream(0), Stream(1)]

    sems = {}
    for k in ("pe", "ve", "se", "gq"):
        sems[k] = sem("s_" + k)
    for i in range(4):
        sems["g%d" % i] = sem("s_g%d" % i)
        sems["a%d" % i] = sem("s_a%d" % i)
    for i in range(NXB):
        sems["x%d" % i] = sem("s_x%d" % i)
        sems["t%d" % i] = sem("s_t%d" % i)
    for i in range(2):
        sems["p%d" % i] = sem("s_p%d" % i)
    C = {k: 0 for k in sems}

    Q = {k: [] for k in ("pe", "ve", "se", "sy", "gp")}

    def op(eng, fn):
        Q[eng].append(fn)

    def wait(eng, s, val):
        if val and val > 0:
            Q[eng].append(lambda e, _s=sems[s], _v=val: e.wait_ge(_s, _v))

    def inc(eng, fn, s, by=1):
        C[s] += by
        Q[eng].append(lambda e, _f=fn, _s=sems[s], _b=by: _f(e).then_inc(_s, _b))
        return C[s]

    def drain(eng):
        Q[eng].append(lambda e: e.drain())

    def wait_tok(eng, tok):
        if tok:
            wait(eng, tok[0], tok[1])

    _rot = {"g": 0, "a": 0, "p": 0}

    def rot_dma(eng, kind, fn, pre_toks=()):
        n = 4 if kind in ("g", "a") else 2
        sname = "%s%d" % (kind, _rot[kind] % n)
        _rot[kind] += 1
        for tk in pre_toks:
            wait_tok(eng, tk)
        wait(eng, sname, C[sname])
        return (sname, inc(eng, fn, sname, 16))

    rot_dma("gp", "g", lambda g: g.dma_start(Wg[:, :], w_gate[:, :]))
    rot_dma("gp", "g", lambda g: g.dma_start(onescol[:, :], c_onescol[:, :]))
    rot_dma("gp", "g", lambda g: g.dma_start(onesrow[:, :], c_onesrow[:, :]))
    rot_dma("gp", "g", lambda g: g.dma_start(id128[:, :], c_id[:, :]))
    rot_dma("gp", "g", lambda g: g.dma_start(oneh[:, :], c_oneh[:, :]))
    rot_dma("gp", "g", lambda g: g.dma_start(IND[:, :], c_ind[:, :]))
    rot_dma("gp", "g", lambda g: g.dma_start(EXPD[:, :], c_expd[:, :]))
    consts_loaded = [("g%d" % i, C["g%d" % i]) for i in range(4)]
    exp_loaded = {}
    for e in range(NE):
        exp_loaded[e] = rot_dma("gp", "g", lambda g, _e=e: g.dma_start(
            Wexp[:, _e * NCH * D:(_e + 1) * NCH * D], w_exp[_e, :, :]))
    weights_loaded = [("g%d" % i, C["g%d" % i]) for i in range(4)]

    bias_tok = rot_dma("sy", "a", lambda s: s.dma_start(Bias[:, :], biases[:, :]))

    precast_done = {}
    attn_pool_free = {}
    wab_free = {}
    for i in range(16):
        m, j2 = i // 4, i % 4
        buf = i % NWB
        ld = rot_dma("gp", "g", lambda g, _m=m, _j=j2, _b=buf: g.dma_start(
            Wab[:, _b * 2048:(_b + 1) * 2048], w_attn[_m, _j, :, :]),
            pre_toks=(wab_free.get(buf),))
        st = rot_dma("sy", "p", lambda s, _m=m, _j=j2, _b=buf: s.dma_start(
            w_attn_bf[_m, _j, :, :], Wab[:, _b * 2048:(_b + 1) * 2048]),
            pre_toks=(ld,))
        precast_done[(m, j2)] = st
        wab_free[buf] = st
        attn_pool_free[buf] = st

    x_loaded = {}
    x_stored = {}

    def emit_x_load(t):
        if t >= nt_count:
            return
        buf = t % NXB
        sname = "x%d" % buf
        wait_tok("sy", x_stored.get(t - NXB))
        wait("sy", sname, C[sname])
        x_loaded[t] = (sname, inc("sy", lambda s, _t=t, _b=buf: s.dma_start(
            XT[_b][:, :], x_in[_t, :, :, :].rearrange("p c n -> p (c n)")), sname, 16))

    for t in range(min(NXB, nt_count)):
        emit_x_load(t)

    blk_loaded = {}

    def emit_blk_load(t, i, idx, buffree_tok):
        m, j2 = i // 4, i % 4
        buf = idx % NWB
        pre = [buffree_tok]
        if t <= 1:
            pre.append(precast_done[(m, j2)])
        blk_loaded[(t, i)] = rot_dma("sy", "a", lambda s, _m=m, _j=j2, _b=buf: s.dma_start(
            Wab[:, _b * 2048:(_b + 1) * 2048], w_attn_bf[_m, _j, :, :]),
            pre_toks=pre)


    def ln_chain(St, t, which, use_gp=False):
        xt = XT[t % NXB]
        sa, yec = St.SA, St.yec
        B0, B1, B3 = St.B[0], St.B[1], St.B[3]
        if which == 1:
            wait_tok("se", x_loaded[t])
            wait_tok("ve", x_loaded[t])
            wait("se", "ve", St.st.get("tile_evac", 0))
        elif which == 2:
            wait("se", "ve", St.st.get("ffn_evac", 0))
        else:
            wait("se", "pe", St.st.get("attn_sa_free_pe", 0))
            wait("ve", "pe", St.st.get("attn_sa_free_pe", 0))
            wait("se", "ve", St.st.get("attn_evac", 0))
        drain("ve")
        cast_ve = inc("ve", lambda v: v.tensor_copy(sa[:, 0:4 * T], xt[:, 0:4 * T]), "ve")
        drain("ve")
        if which == 2:
            cast_se = ("ve", inc("ve", lambda v: v.tensor_copy(
                sa[:, 4 * T:8 * T], xt[:, 4 * T:8 * T]), "ve"))
            drain("ve")
        else:
            cast_se = ("se", inc("se", lambda s: s.copy(sa[:, 4 * T:8 * T], xt[:, 4 * T:8 * T]), "se"))
        yield
        wait("pe", "ve", cast_ve)
        for c in range(NCH):
            if c == 4:
                wait_tok("pe", cast_se)
            fn = lambda tt, _c=c: tt.matmul(B3[0:1, 0:T], onescol[:, :],
                                            sa[:, _c * T:(_c + 1) * T],
                                            start=(_c == 0), stop=(_c == NCH - 1))
            if c == NCH - 1:
                inc("pe", fn, "pe")
            else:
                op("pe", fn)
        yield
        mm_sq = {}
        sq_done = {}

        def emit_sq(cp):
            pp = cp % 2
            if cp == 0:
                wait("se", "ve", cast_ve)
            if cp == 2:
                wait_tok("se", cast_se)
            if cp >= 2:
                wait("se", "pe", mm_sq.get(2 * cp - 3, 0))
            vv = sa[:, 2 * cp * T:(2 * cp + 2) * T]
            sq_done[cp] = inc("se", lambda s, _vv=vv, _p=pp: s.activation(
                yec[:, _p * 2 * T:(_p + 1) * 2 * T], _vv, ACT.Square), "se")
            drain("se")

        def emit_mmsq(c):
            cp = c // 2
            slot = (cp % 2) * 2 + (c % 2)
            wait("pe", "se", sq_done[cp])
            mm_sq[c] = inc("pe", lambda tt, _c=c, _s=slot: tt.matmul(
                B3[32:33, 0:T], onescol[:, :], yec[:, _s * T:(_s + 1) * T],
                start=(_c == 0), stop=(_c == NCH - 1)), "pe")

        emit_sq(0)
        emit_sq(1)
        yield
        for c in range(NCH):
            emit_mmsq(c)
            if c % 2 == 1 and c // 2 + 2 < NCH // 2:
                emit_sq(c // 2 + 2)
            if c % 2 == 1:
                yield
        sumsq_done = mm_sq[NCH - 1]
        wait("ve", "pe", sumsq_done)
        op("ve", lambda v: v.tensor_scalar(St.Msb[:, :], B3[0:1, 0:T], INV_D, None, ALU.mult))
        op("ve", lambda v: v.tensor_scalar(B3[64:65, 0:T], B3[32:33, 0:T], INV_D, EPS, ALU.mult, ALU.add))
        drain("ve")
        op("ve", lambda v: v.tensor_mul(St.MRb[:, :], St.Msb[:, :], St.Msb[:, :]))
        drain("ve")
        statv_done = inc("ve", lambda v: v.tensor_sub(B3[64:65, 0:T], B3[64:65, 0:T], St.MRb[:, :]), "ve")
        drain("ve")
        mb_done = inc("ve", lambda v: v.tensor_copy(St.MRb[:, :], St.Msb[:, :]), "ve")
        yield
        wait("se", "ve", statv_done)
        op("se", lambda s: s.activation(B3[96:97, 0:T], B3[64:65, 0:T], ACT.Ln))
        drain("se")
        op("se", lambda s: s.activation(B3[64:65, 0:T], B3[96:97, 0:T], ACT.Exp, scale=-0.5))
        drain("se")
        r_done = inc("se", lambda s: s.copy(St.Rb[:, :], B3[64:65, 0:T]), "se")
        yield
        wait("pe", "se", r_done)
        wait("pe", "ve", mb_done)
        op("pe", lambda tt: tt.matmul(B0[:, 0:T], onesrow[:, :], St.Rb[:, :], start=True, stop=True))
        bc_done = inc("pe", lambda tt: tt.matmul(B1[:, 0:T], onesrow[:, :], St.MRb[:, :], start=True, stop=True), "pe")
        yield
        wait("se", "pe", bc_done)
        op("se", lambda s: s.copy(St.rbb[:, :], B0[:, 0:T]))
        drain("se")
        bcv_done = inc("se", lambda s: s.copy(St.mrbb[:, :], B1[:, 0:T]), "se")
        drain("se")
        wait("ve", "se", bcv_done)
        y_pair = {}
        for cp in range(NCH // 2):
            pp = cp % 2
            sv = sa[:, 2 * cp * T:(2 * cp + 2) * T].rearrange("p (h n) -> p h n", h=2)
            yv = yec[:, pp * 2 * T:(pp + 1) * 2 * T].rearrange("p (h n) -> p h n", h=2)
            op("ve", lambda v, _s=sv, _y=yv: v.tensor_sub(
                _y, _s, St.mrbb[:, :].unsqueeze(1).to_broadcast([128, 2, T])))
            drain("ve")
            y_pair[cp] = inc("ve", lambda v, _s=sv, _y=yv: v.tensor_mul(
                _s, _y, St.rbb[:, :].unsqueeze(1).to_broadcast([128, 2, T])), "ve")
            drain("ve")
            if cp % 2 == 1:
                yield
        St.st["y_pair"] = y_pair
        St.st["y_done"] = y_pair[NCH // 2 - 1]

    def gate_chain(St):
        sa, B2 = St.SA, St.B[2]
        y_done = St.st["y_done"]
        y_pair = St.st["y_pair"]
        logits_done = None
        for c in range(NCH):
            if c % 2 == 0:
                wait("pe", "ve", y_pair[c // 2])
            fn = lambda tt, _c=c: tt.matmul(
                B2[0:8, 0:T], Wg[:, _c * NE:(_c + 1) * NE],
                sa[:, _c * T:(_c + 1) * T],
                start=(_c == 0), stop=(_c == NCH - 1))
            if c == NCH - 1:
                logits_done = inc("pe", fn, "pe")
            else:
                op("pe", fn)
        yield
        wait("se", "pe", logits_done)
        cf_ev = inc("se", lambda s: s.copy(St.Cf[0:8, :], B2[0:8, 0:T]), "se")
        drain("se")
        yield
        wait("pe", "se", cf_ev)
        trl_done = None
        for g in range(NG):
            fn = lambda tt, _g=g: tt.transpose(
                St.B2b[0:128, _g * 8:(_g + 1) * 8],
                St.Cf[0:8, _g * 128:(_g + 1) * 128], id128[0:8, 0:8])
            if g == NG - 1:
                trl_done = inc("pe", fn, "pe")
            else:
                op("pe", fn)
        yield
        wait("se", "pe", trl_done)
        lg_done = inc("se", lambda s: s.copy(St.Lg[:, :], St.B2b[:, 0:NG * 8]), "se")
        drain("se")
        wait("ve", "se", lg_done)
        for g in range(NG):
            op("ve", lambda v, _g=g: v.max(out=St.M8[:, _g * 8:(_g + 1) * 8],
                                           in_=St.Lg[:, _g * 8:(_g + 1) * 8]))
        drain("ve")
        m8r = St.M8[:, 0:NG * 8].rearrange("p (g e) -> p g e", g=NG)
        lgr = St.Lg[:, :].rearrange("p (g e) -> p g e", g=NG)
        op("ve", lambda v: v.tensor_sub(
            St.ELg[:, 16:16 + NG], m8r[:, :, 1:2], m8r[:, :, 0:1]))
        op("ve", lambda v: v.tensor_sub(St.ELg[:, 0:NG * 8].rearrange("p (g e) -> p g e", g=NG),
                                        lgr, m8r[:, :, 0:1].to_broadcast([128, NG, 8])))
        dl_done = inc("ve", lambda v: v.tensor_tensor(
            St.SELg[:, :].rearrange("p (g e) -> p g e", g=NG), lgr,
            m8r[:, :, 1:2].to_broadcast([128, NG, 8]), ALU.is_ge), "ve")
        yield
        wait("se", "ve", dl_done)
        el_done = inc("se", lambda s: s.activation(St.M8[:, :], St.ELg[:, :], ACT.Exp), "se")
        yield
        wait("ve", "se", el_done)
        op("ve", lambda v: v.tensor_scalar(St.D2[:, :], St.M8[:, 16:16 + NG], 1.0, 2.0, ALU.add, ALU.mult))
        drain("ve")
        op("ve", lambda v: v.reciprocal(St.RR[:, :], St.D2[:, :]))
        drain("ve")
        op("ve", lambda v: v.tensor_tensor(
            St.ELg[:, 0:NG * 8].rearrange("p (g e) -> p g e", g=NG),
            St.M8[:, 0:NG * 8].rearrange("p (g e) -> p g e", g=NG),
            St.RR[:, :].unsqueeze(2).to_broadcast([128, NG, 8]), ALU.mult))
        drain("ve")
        wtm_done = inc("ve", lambda v: v.tensor_mul(
            St.wtm[:, :], St.ELg[:, 0:NG * 8], St.SELg[:, :]), "ve")
        yield
        wait("pe", "ve", wtm_done)
        tr_done = None
        for g in range(NG):
            fn = lambda tt, _g=g: tt.transpose(
                St.B3b[0:8, _g * 128:(_g + 1) * 128], St.wtm[:, _g * 8:(_g + 1) * 8], id128[:, :])
            if g == NG - 1:
                tr_done = inc("pe", fn, "pe")
            else:
                op("pe", fn)
        yield
        wait("ve", "pe", tr_done)
        wem_done = inc("ve", lambda v: v.tensor_copy(St.w_em[:, :], St.B3b[0:8, 0:NG * 128]), "ve")
        yield
        wait("pe", "ve", wem_done)
        wbc_evac = {}
        for e in range(NE):
            bank = St.B[e % 4]
            if e >= 4:
                wait("pe", "se", wbc_evac[e - 4])
            mmv = inc("pe", lambda tt, _b=bank, _e=e: tt.matmul(
                _b[:, 0:T], oneh[0:8, _e * 128:(_e + 1) * 128], St.w_em[0:8, :],
                start=True, stop=True), "pe")
            wait("se", "pe", mmv)
            wbc_evac[e] = inc("se", lambda s, _b=bank, _e=e: s.copy(
                St.SC[:, _e * T:(_e + 1) * T], _b[:, 0:T]), "se")
            drain("se")
            if e % 2 == 1:
                yield
        St.st["wbc_done"] = ("se", wbc_evac[NE - 1])
        St.st["wbc_e"] = {e: ("se", wbc_evac[e]) for e in range(NE)}


    def ffn_phase(St, t, first_ffn=False):
        xt = XT[t % NXB]
        sa, sc, yec = St.SA, St.SC, St.yec
        wait_tok("pe", St.st["wbc_done"])
        mm_done = {}
        i = 0
        for e in range(NE):
            if first_ffn:
                wait_tok("pe", exp_loaded[e])
            for cp in range(NCH // 2):
                c0 = 2 * cp
                pp = i % 4
                if cp == 0:
                    wait_tok("gp", St.st["wbc_e"][e])
                if i >= 4:
                    wait("gp", "pe", mm_done[i - 4])
                yd = inc("gp", lambda g, _c=c0, _e=e, _p=pp: g.tensor_mul(
                    yec[:, _p * 2 * T:(_p + 1) * 2 * T].rearrange("p (h n) -> p h n", h=2),
                    sa[:, _c * T:(_c + 2) * T].rearrange("p (h n) -> p h n", h=2),
                    sc[:, _e * T:(_e + 1) * T].unsqueeze(1).to_broadcast([128, 2, T])), "gq")
                wait("pe", "gq", yd)
                for half in range(2):
                    c = c0 + half
                    for dj in range(8):
                        bank = St.B[dj % 4]
                        lo = (dj // 4) * 256
                        fn = lambda tt, _e=e, _c=c, _dj=dj, _p=pp, _h=half, _bk=bank, _lo=lo: tt.matmul(
                            _bk[:, _lo:_lo + T],
                            Wexp[:, ((_e * NCH + _c) * D + _dj * 128):((_e * NCH + _c) * D + (_dj + 1) * 128)],
                            yec[:, (_p * 2 + _h) * T:(_p * 2 + _h + 1) * T],
                            start=(_e == 0 and _c == 0 and _dj < 4),
                            stop=(_e == NE - 1 and _c == NCH - 1 and _dj >= 4))
                        if half == 1 and dj == 7:
                            mm_done[i] = inc("pe", fn, "pe")
                        else:
                            op("pe", fn)
                i += 1
                yield
        ffn_mm_done = mm_done[31]
        wait("ve", "pe", ffn_mm_done)
        evac = None
        for bkid in range(4):
            bank = St.B[bkid]
            xv = xt[:, :].rearrange("p (h c n) -> p h c n", h=2, c=4)[:, :, bkid, :]
            evac = inc("ve", lambda v, _bk=bank, _xv=xv: v.tensor_tensor(
                _xv, _xv, _bk[:, :].rearrange("p (h n) -> p h n", h=2), ALU.add), "ve")
        St.st["ffn_evac"] = evac

    def attn_phase(St, t):
        xt = XT[t % NXB]
        sa, sc, yec = St.SA, St.SC, St.yec
        B0, B1, B2 = St.B[0], St.B[1], St.B[2]
        y_done = St.st["y_done"]
        blk_free = {}

        def lhs(i, c, dj):
            buf = i % NWB
            base = buf * 2048 + c * 256 + (dj % 2) * 128
            return Wab[:, base:base + 128]

        order = []
        for j2 in range(4):
            order += [0 * 4 + j2, 1 * 4 + j2]
        order += [2 * 4 + j2 for j2 in range(4)]
        order += [3 * 4 + j2 for j2 in range(4)]
        pos = {b: idx for idx, b in enumerate(order)}

        def load_for(idx):
            if idx >= 16:
                return
            b = order[idx]
            if idx - NWB >= 0:
                tok = ("pe", blk_free[order[idx - NWB]])
            else:
                tok = attn_pool_free.get(idx % NWB)
            emit_blk_load(t, b, idx, tok)

        for idx in range(min(NWB, 16)):
            load_for(idx)

        wait("pe", "ve", y_done)
        qk_done = {}
        se_prev = {}
        se_q = {}
        q_read = {}
        k_read = {}
        for j2 in range(4):
            qb, kb = 0 * 4 + j2, 1 * 4 + j2
            wait_tok("pe", blk_loaded[(t, qb)])
            wait_tok("pe", blk_loaded[(t, kb)])
            for dj in (2 * j2, 2 * j2 + 1):
                if dj >= 1:
                    wait("pe", "se", q_read[dj - 1])
                for c in range(NCH):
                    fn = lambda tt, _c=c, _dj=dj, _i=pos[qb]: tt.matmul(
                        B0[:, 0:T], lhs(_i, _c, _dj), sa[:, _c * T:(_c + 1) * T],
                        start=(_c == 0), stop=(_c == NCH - 1))
                    if c == NCH - 1:
                        qmm = inc("pe", fn, "pe")
                    else:
                        op("pe", fn)
                if dj >= 1:
                    wait("pe", "ve", k_read[dj - 1])
                for c in range(NCH):
                    fn = lambda tt, _c=c, _dj=dj, _i=pos[kb]: tt.matmul(
                        B1[:, 0:T], lhs(_i, _c, _dj), sa[:, _c * T:(_c + 1) * T],
                        start=(_c == 0), stop=(_c == NCH - 1))
                    if c == NCH - 1:
                        kmm = inc("pe", fn, "pe")
                    else:
                        op("pe", fn)
                wait("se", "pe", qmm)
                if dj >= 1:
                    wait("se", "ve", qk_done[dj - 1])
                q_read[dj] = inc("se", lambda s, _dj=dj: s.activation(
                    yec[:, 0:T], B0[:, 0:T], ACT.Square,
                    bias=Bias[:, 0 * 8 + _dj:0 * 8 + _dj + 1]), "se")
                drain("se")
                op("se", lambda s: s.activation(yec[:, T:2 * T], yec[:, 0:T], ACT.Exp, scale=-0.5))
                drain("se")
                wait("ve", "pe", kmm)
                if dj >= 1:
                    wait("ve", "se", se_prev[dj - 1])
                k_read[dj] = inc("ve", lambda v, _dj=dj: v.tensor_scalar(
                    yec[:, 2 * T:3 * T], B1[:, 0:T], Bias[:, 1 * 8 + _dj:1 * 8 + _dj + 1], None, ALU.add), "ve")
                drain("ve")
                sq_done = inc("ve", lambda v: v.tensor_mul(
                    yec[:, 2 * T:3 * T], yec[:, 2 * T:3 * T], yec[:, 2 * T:3 * T]), "ve")
                wait("se", "ve", sq_done)
                se_prev[dj] = inc("se", lambda s: s.activation(
                    yec[:, 3 * T:4 * T], yec[:, 2 * T:3 * T], ACT.Exp, scale=-0.5), "se")
                drain("se")
                wait("ve", "se", se_prev[dj])
                qk_done[dj] = inc("ve", lambda v, _dj=dj: v.tensor_mul(
                    sc[:, _dj * T:(_dj + 1) * T], yec[:, T:2 * T], yec[:, 3 * T:4 * T]), "ve")
                yield
            blk_free[qb] = C["pe"]
            blk_free[kb] = C["pe"]
            load_for(pos[qb] + NWB)
            load_for(pos[kb] + NWB)
        wait("pe", "ve", qk_done[7])
        coeff_done = None
        for c in range(NCH):
            fn = lambda tt, _c=c: tt.matmul(
                B2[0:16, 0:T], IND[:, _c * 16:(_c + 1) * 16], sc[:, _c * T:(_c + 1) * T],
                start=(_c == 0), stop=(_c == NCH - 1))
            if c == NCH - 1:
                coeff_done = inc("pe", fn, "pe")
            else:
                op("pe", fn)
        wait("ve", "pe", coeff_done)
        cf_done = inc("ve", lambda v: v.tensor_copy(St.Cf[:, :], B2[0:16, 0:T]), "ve")
        yield
        vevac = {}
        for dj in range(8):
            b = 2 * 4 + dj // 2
            bank = B0 if dj % 2 == 0 else B1
            if dj % 2 == 0:
                wait_tok("pe", blk_loaded[(t, b)])
            if dj >= 2:
                wait("pe", "ve", vevac[dj - 2])
            for c in range(NCH):
                fn = lambda tt, _c=c, _dj=dj, _i=pos[b], _bk=bank: tt.matmul(
                    _bk[:, 0:T], lhs(_i, _c, _dj), sa[:, _c * T:(_c + 1) * T],
                    start=(_c == 0), stop=(_c == NCH - 1))
                if c == NCH - 1:
                    vmm = inc("pe", fn, "pe")
                else:
                    op("pe", fn)
            if dj % 2 == 1:
                blk_free[b] = C["pe"]
                load_for(pos[b] + NWB)
            cbank = St.B[2 + dj % 2]
            if dj == 0:
                wait("pe", "ve", cf_done)
            if dj >= 2:
                wait("pe", "ve", vevac[dj - 2])
            expd = inc("pe", lambda tt, _dj=dj, _cb=cbank: tt.matmul(
                _cb[:, 0:T], EXPD[0:16, _dj * 128:(_dj + 1) * 128], St.Cf[0:16, :],
                start=True, stop=True), "pe")
            wait("ve", "pe", vmm)
            wait("ve", "pe", expd)
            op("ve", lambda v, _dj=dj, _bk=bank: v.tensor_scalar(
                yec[:, 0:T], _bk[:, 0:T],
                Bias[:, 2 * 8 + _dj:2 * 8 + _dj + 1], None, ALU.add))
            drain("ve")
            vevac[dj] = inc("ve", lambda v, _dj=dj, _cb=cbank: v.scalar_tensor_tensor(
                sc[:, _dj * T:(_dj + 1) * T], _cb[:, 0:T], 1.0,
                yec[:, 0:T], ALU.bypass, ALU.mult), "ve")
            drain("ve")
            yield
        St.st["attn_sa_free_pe"] = C["pe"]
        oevac = {}
        wait("pe", "ve", vevac[7])
        for dj in range(8):
            b = 3 * 4 + dj // 2
            bank = B0 if dj % 2 == 0 else B1
            if dj % 2 == 0:
                wait_tok("pe", blk_loaded[(t, b)])
            if dj >= 2:
                wait("pe", "ve", oevac[dj - 2])
            for c in range(NCH):
                fn = lambda tt, _c=c, _dj=dj, _i=pos[b], _bk=bank: tt.matmul(
                    _bk[:, 0:T], lhs(_i, _c, _dj), sc[:, _c * T:(_c + 1) * T],
                    start=(_c == 0), stop=(_c == NCH - 1))
                if c == NCH - 1:
                    omm = inc("pe", fn, "pe")
                else:
                    op("pe", fn)
            if dj % 2 == 1:
                blk_free[b] = C["pe"]
            wait("ve", "pe", omm)
            oevac[dj] = inc("ve", lambda v, _dj=dj, _bk=bank: v.scalar_tensor_tensor(
                xt[:, _dj * T:(_dj + 1) * T], _bk[:, 0:T],
                Bias[:, 3 * 8 + _dj:3 * 8 + _dj + 1], xt[:, _dj * T:(_dj + 1) * T],
                ALU.add, ALU.add), "ve")
            yield
        St.st["attn_evac"] = oevac[7]
        for k in range(4):
            attn_pool_free[(12 + k) % NWB] = ("pe", blk_free[order[12 + k]])


    def run(heavy, chain, delay=2, stride=1):
        k = 0
        for _ in heavy:
            k += 1
            if (chain is not None and k >= delay
                    and (k - delay) % stride == 0):
                try:
                    next(chain)
                except StopIteration:
                    chain = None
        while chain is not None:
            try:
                next(chain)
            except StopIteration:
                chain = None

    def seq(*gens):
        for g in gens:
            for x in g:
                yield x

    def full(gen):
        for _ in gen:
            pass

    def store_tile(St, t):
        buf = t % NXB
        sname = "t%d" % buf
        wait("sy", sname, C[sname])
        wait("sy", "ve", St.st["ffn_evac"])
        x_stored[t] = (sname, inc("sy", lambda s, _t=t, _b=buf: s.dma_start(
            x_out[_t, :, :, :].rearrange("p c n -> p (c n)"), XT[_b][:, :]), sname, 16))
        St.st["tile_evac"] = St.st["ffn_evac"]
        emit_x_load(t + NXB)

    for tok in consts_loaded:
        wait_tok("pe", tok)
    wait_tok("ve", bias_tok)
    wait_tok("se", bias_tok)

    npair = nt_count // 2
    full(seq(ln_chain(S[0], 0, 1), gate_chain(S[0])))
    for p in range(npair):
        a, b = 2 * p, 2 * p + 1
        A, Bs = S[0], S[1]
        run(ffn_phase(A, a, first_ffn=(p == 0)), seq(ln_chain(Bs, b, 1), gate_chain(Bs)))
        run(ffn_phase(Bs, b, first_ffn=(p == 0)), ln_chain(A, a, 2), delay=4, stride=2)
        run(attn_phase(A, a), ln_chain(Bs, b, 2, use_gp=True), delay=4, stride=2)
        run(attn_phase(Bs, b), seq(ln_chain(A, a, 3, use_gp=True), gate_chain(A)), delay=2, stride=1)
        run(ffn_phase(A, a), seq(ln_chain(Bs, b, 3), gate_chain(Bs)))
        store_tile(A, a)
        nxt = seq(ln_chain(A, 2 * p + 2, 1), gate_chain(A)) if p + 1 < npair else None
        run(ffn_phase(Bs, b), nxt)
        store_tile(Bs, b)

    for i in range(NXB):
        wait("sy", "t%d" % i, C["t%d" % i])

    blk = ctx.enter_context(nc.Block())

    @blk.tensor
    def _(t):
        for fn in Q["pe"]:
            fn(t)

    @blk.vector
    def _(v):
        for fn in Q["ve"]:
            fn(v)

    @blk.scalar
    def _(s):
        for fn in Q["se"]:
            fn(s)

    @blk.sync
    def _(s):
        for fn in Q["sy"]:
            fn(s)

    @blk.gpsimd
    def _(g):
        for fn in Q["gp"]:
            fn(g)

    ctx.close()
    return nc


def _pack_inputs(L0_attn_x, L0_attn_Wq, L0_attn_bq, L0_attn_Wk, L0_attn_bk,
                 L0_attn_Wv, L0_attn_bv, L0_attn_Wo, L0_attn_bo,
                 L0_ffn_gate_w, L0_ffn_expert_w):
    x = np.ascontiguousarray(np.asarray(L0_attn_x, dtype=np.float32)).reshape(-1, D)
    n_tok = x.shape[0]
    per = n_tok // NCORES

    ew = np.ascontiguousarray(np.asarray(L0_ffn_expert_w, dtype=np.float32))
    ewT = ew.transpose(0, 2, 1)
    w_exp = np.ascontiguousarray(
        ewT.reshape(NE, NCH, 128, D).transpose(0, 2, 1, 3).reshape(NE, 128, NCH * D))

    def attn_pack(W):
        W = np.asarray(W, dtype=np.float32)
        return W.reshape(NCH, 128, 4, 256).transpose(2, 1, 0, 3).reshape(4, 128, NCH * 256)
    w_attn = np.ascontiguousarray(np.stack(
        [attn_pack(L0_attn_Wq), attn_pack(L0_attn_Wk),
         attn_pack(L0_attn_Wv), attn_pack(L0_attn_Wo)]))

    bs = np.stack([np.asarray(bv, dtype=np.float32) for bv in
                   (L0_attn_bq, L0_attn_bk, L0_attn_bv, L0_attn_bo)])
    biases = np.ascontiguousarray(bs.reshape(4, NCH, 128).transpose(2, 0, 1).reshape(128, 32))

    gw = np.asarray(L0_ffn_gate_w, dtype=np.float32)
    w_gate = np.ascontiguousarray(
        gw.T.reshape(NCH, 128, NE).transpose(1, 0, 2).reshape(128, NCH * NE))

    c_onescol = np.ones((128, 1), np.float32)
    c_onesrow = np.ones((1, 128), np.float32)
    c_id = np.eye(128, dtype=np.float32)
    c_oneh = np.zeros((8, 1024), np.float32)
    for e in range(8):
        c_oneh[e, e * 128:(e + 1) * 128] = 1.0
    c_ind = np.zeros((128, 128), np.float32)
    for c in range(NCH):
        for pp in range(128):
            c_ind[pp, c * 16 + 2 * c + pp // 64] = 1.0
    c_expd = np.zeros((16, 1024), np.float32)
    for dj in range(NCH):
        for pp in range(128):
            c_expd[2 * dj + pp // 64, dj * 128 + pp] = 1.0

    shared = {
        "w_exp": w_exp, "w_attn": w_attn, "w_gate": w_gate, "biases": biases,
        "c_onescol": c_onescol, "c_onesrow": c_onesrow, "c_id": c_id,
        "c_oneh": c_oneh, "c_ind": c_ind, "c_expd": c_expd,
    }
    in_maps = []
    for ci in range(NCORES):
        xc = x[ci * per:(ci + 1) * per]
        x_in = np.ascontiguousarray(
            xc.reshape(NT, T, NCH, 128).transpose(0, 3, 2, 1))
        m = dict(shared)
        m["x_in"] = x_in
        in_maps.append(m)
    return in_maps


def kernel(L0_attn_x, L0_attn_Wq, L0_attn_bq, L0_attn_Wk, L0_attn_bk,
           L0_attn_Wv, L0_attn_bv, L0_attn_Wo, L0_attn_bo,
           L0_ffn_gate_w, L0_ffn_expert_w):
    in_maps = _pack_inputs(L0_attn_x, L0_attn_Wq, L0_attn_bq, L0_attn_Wk,
                           L0_attn_bk, L0_attn_Wv, L0_attn_bv, L0_attn_Wo,
                           L0_attn_bo, L0_ffn_gate_w, L0_ffn_expert_w)
    nc = build_program()
    res = run_bass_kernel_spmd(nc, in_maps, core_ids=list(range(NCORES)))
    outs = []
    for ci in range(NCORES):
        o = np.asarray(res.results[ci]["x_out"])
        outs.append(o.transpose(0, 3, 2, 1).reshape(NT * T, D))
    full_out = np.concatenate(outs, axis=0)
    Bb, Sq = np.asarray(L0_attn_x).shape[0], np.asarray(L0_attn_x).shape[1]
    return np.ascontiguousarray(full_out.reshape(Bb, Sq, D).astype(np.float32))


# version 29
# speedup vs baseline: 1.0149x; 1.0149x over previous
import numpy as np
from contextlib import ExitStack

import concourse.bass as bass
import concourse.mybir as mybir
from concourse.bass_utils import run_bass_kernel_spmd

F32 = mybir.dt.float32
BF16 = mybir.dt.bfloat16
ALU = mybir.AluOpType
ACT = mybir.ActivationFunctionType

NCORES = 8
NT = 16
T = 256
NG = T // 128
NCH = 8
NE = 8
D = 1024
EPS = 1e-5
INV_D = 1.0 / D
NXB = 3
NWB = 4


def build_program(nt_count=NT):
    nc = bass.Bass("TRN2", target_bir_lowering=False)

    x_in = nc.dram_tensor("x_in", [NT, 128, NCH, T], F32, kind="ExternalInput")
    x_out = nc.dram_tensor("x_out", [NT, 128, NCH, T], F32, kind="ExternalOutput")
    w_exp = nc.dram_tensor("w_exp", [NE, 128, NCH * D], F32, kind="ExternalInput")
    w_attn = nc.dram_tensor("w_attn", [4, 4, 128, NCH * 256], F32, kind="ExternalInput")
    w_gate = nc.dram_tensor("w_gate", [128, NCH * NE], F32, kind="ExternalInput")
    biases = nc.dram_tensor("biases", [128, 32], F32, kind="ExternalInput")
    c_onescol = nc.dram_tensor("c_onescol", [128, 1], F32, kind="ExternalInput")
    c_onesrow = nc.dram_tensor("c_onesrow", [1, 128], F32, kind="ExternalInput")
    c_id = nc.dram_tensor("c_id", [128, 128], F32, kind="ExternalInput")
    c_oneh = nc.dram_tensor("c_oneh", [8, 1024], F32, kind="ExternalInput")
    c_ind = nc.dram_tensor("c_ind", [128, 128], F32, kind="ExternalInput")
    c_expd = nc.dram_tensor("c_expd", [16, 1024], F32, kind="ExternalInput")
    w_attn_bf = nc.dram_tensor("w_attn_bf", [4, 4, 128, NCH * 256], BF16)

    ctx = ExitStack()
    sb = lambda name, shape, dt: ctx.enter_context(nc.sbuf_tensor(name, shape, dt))
    ps = lambda name, shape, dt: ctx.enter_context(nc.psum_tensor(name, shape, dt))
    sem = lambda name: ctx.enter_context(nc.semaphore(name))

    Wexp = sb("Wexp", [128, NE * NCH * D], BF16)
    Wab = sb("Wab", [128, NWB * NCH * 256], BF16)
    Wg = sb("Wg", [128, NCH * NE], BF16)
    Bias = sb("Bias", [128, 32], F32)
    onescol = sb("onescol", [128, 1], BF16)
    onesrow = sb("onesrow", [1, 128], BF16)
    id128 = sb("id128", [128, 128], BF16)
    oneh = sb("oneh", [8, 1024], BF16)
    IND = sb("IND", [128, 128], BF16)
    EXPD = sb("EXPD", [16, 1024], BF16)
    XT = [sb(f"XT{i}", [128, NCH * T], F32) for i in range(NXB)]

    P = [ps(f"P{i}", [128, 512], F32) for i in range(8)]

    class Stream:
        def __init__(self, sid):
            self.sid = sid
            tag = "AB"[sid]
            self.SA = sb("SA" + tag, [128, NCH * T], BF16)
            self.SC = sb("SC" + tag, [128, NCH * T], BF16)
            self.yec = sb("yec" + tag, [128, 8 * T], BF16)
            self.rbb = sb("rbb" + tag, [128, T], BF16)
            self.mrbb = sb("mrbb" + tag, [128, T], BF16)
            self.Msb = sb("Msb" + tag, [1, T], F32)
            self.Rb = sb("Rb" + tag, [1, T], BF16)
            self.MRb = sb("MRb" + tag, [1, T], BF16)
            self.Lg = sb("Lg" + tag, [128, NG * 8], F32)
            self.M8 = sb("M8" + tag, [128, NG * 8 + NG], F32)
            self.ELg = sb("ELg" + tag, [128, NG * 8 + NG], F32)
            self.SELg = sb("SELg" + tag, [128, NG * 8], F32)
            self.wtm = sb("wtm" + tag, [128, NG * 8], BF16)
            self.D2 = sb("D2" + tag, [128, NG], F32)
            self.RR = sb("RR" + tag, [128, NG], F32)
            self.w_em = sb("wem" + tag, [8, T], BF16)
            self.Cf = sb("Cf" + tag, [16, T], BF16)
            self.B = P[4 * sid:4 * sid + 4]
            self.B3b = P[4 * sid + 3][:, :].bitcast(BF16)
            self.B2b = P[4 * sid + 2][:, :].bitcast(BF16)
            self.st = {}

    S = [Stream(0), Stream(1)]

    sems = {}
    for k in ("pe", "ve", "se", "gq"):
        sems[k] = sem("s_" + k)
    for i in range(4):
        sems["g%d" % i] = sem("s_g%d" % i)
        sems["a%d" % i] = sem("s_a%d" % i)
    for i in range(NXB):
        sems["x%d" % i] = sem("s_x%d" % i)
        sems["t%d" % i] = sem("s_t%d" % i)
    for i in range(2):
        sems["p%d" % i] = sem("s_p%d" % i)
    C = {k: 0 for k in sems}

    Q = {k: [] for k in ("pe", "ve", "se", "sy", "gp")}

    def op(eng, fn):
        Q[eng].append(fn)

    def wait(eng, s, val):
        if val and val > 0:
            Q[eng].append(lambda e, _s=sems[s], _v=val: e.wait_ge(_s, _v))

    def inc(eng, fn, s, by=1):
        C[s] += by
        Q[eng].append(lambda e, _f=fn, _s=sems[s], _b=by: _f(e).then_inc(_s, _b))
        return C[s]

    def drain(eng):
        Q[eng].append(lambda e: e.drain())

    def wait_tok(eng, tok):
        if tok:
            wait(eng, tok[0], tok[1])

    _rot = {"g": 0, "a": 0, "p": 0}

    def rot_dma(eng, kind, fn, pre_toks=()):
        n = 4 if kind in ("g", "a") else 2
        sname = "%s%d" % (kind, _rot[kind] % n)
        _rot[kind] += 1
        for tk in pre_toks:
            wait_tok(eng, tk)
        wait(eng, sname, C[sname])
        return (sname, inc(eng, fn, sname, 16))

    rot_dma("gp", "g", lambda g: g.dma_start(Wg[:, :], w_gate[:, :]))
    rot_dma("gp", "g", lambda g: g.dma_start(onescol[:, :], c_onescol[:, :]))
    rot_dma("gp", "g", lambda g: g.dma_start(onesrow[:, :], c_onesrow[:, :]))
    rot_dma("gp", "g", lambda g: g.dma_start(id128[:, :], c_id[:, :]))
    rot_dma("gp", "g", lambda g: g.dma_start(oneh[:, :], c_oneh[:, :]))
    rot_dma("gp", "g", lambda g: g.dma_start(IND[:, :], c_ind[:, :]))
    rot_dma("gp", "g", lambda g: g.dma_start(EXPD[:, :], c_expd[:, :]))
    consts_loaded = [("g%d" % i, C["g%d" % i]) for i in range(4)]
    exp_loaded = {}
    for e in range(NE):
        exp_loaded[e] = rot_dma("gp", "g", lambda g, _e=e: g.dma_start(
            Wexp[:, _e * NCH * D:(_e + 1) * NCH * D], w_exp[_e, :, :]))
    weights_loaded = [("g%d" % i, C["g%d" % i]) for i in range(4)]

    bias_tok = rot_dma("sy", "a", lambda s: s.dma_start(Bias[:, :], biases[:, :]))

    precast_done = {}
    attn_pool_free = {}
    wab_free = {}
    for i in range(16):
        m, j2 = i // 4, i % 4
        buf = i % NWB
        ld = rot_dma("gp", "g", lambda g, _m=m, _j=j2, _b=buf: g.dma_start(
            Wab[:, _b * 2048:(_b + 1) * 2048], w_attn[_m, _j, :, :]),
            pre_toks=(wab_free.get(buf),))
        st = rot_dma("sy", "p", lambda s, _m=m, _j=j2, _b=buf: s.dma_start(
            w_attn_bf[_m, _j, :, :], Wab[:, _b * 2048:(_b + 1) * 2048]),
            pre_toks=(ld,))
        precast_done[(m, j2)] = st
        wab_free[buf] = st
        attn_pool_free[buf] = st

    x_loaded = {}
    x_stored = {}

    def emit_x_load(t):
        if t >= nt_count:
            return
        buf = t % NXB
        sname = "x%d" % buf
        wait_tok("sy", x_stored.get(t - NXB))
        wait("sy", sname, C[sname])
        x_loaded[t] = (sname, inc("sy", lambda s, _t=t, _b=buf: s.dma_start(
            XT[_b][:, :], x_in[_t, :, :, :].rearrange("p c n -> p (c n)")), sname, 16))

    for t in range(min(NXB, nt_count)):
        emit_x_load(t)

    blk_loaded = {}

    def emit_blk_load(t, i, idx, buffree_tok):
        m, j2 = i // 4, i % 4
        buf = idx % NWB
        pre = [buffree_tok]
        if t <= 1:
            pre.append(precast_done[(m, j2)])
        blk_loaded[(t, i)] = rot_dma("sy", "a", lambda s, _m=m, _j=j2, _b=buf: s.dma_start(
            Wab[:, _b * 2048:(_b + 1) * 2048], w_attn_bf[_m, _j, :, :]),
            pre_toks=pre)


    def ln_chain(St, t, which, use_gp=False):
        xt = XT[t % NXB]
        sa, yec = St.SA, St.yec
        B0, B1, B3 = St.B[0], St.B[1], St.B[3]
        if which == 1:
            wait_tok("se", x_loaded[t])
            wait_tok("ve", x_loaded[t])
            wait("se", "ve", St.st.get("tile_evac", 0))
        elif which == 2:
            wait("se", "ve", St.st.get("ffn_evac", 0))
        else:
            wait("se", "pe", St.st.get("attn_sa_free_pe", 0))
            wait("ve", "pe", St.st.get("attn_sa_free_pe", 0))
            wait("se", "ve", St.st.get("attn_evac", 0))
        drain("ve")
        cast_ve = inc("ve", lambda v: v.tensor_copy(sa[:, 0:4 * T], xt[:, 0:4 * T]), "ve")
        drain("ve")
        if which == 2:
            cast_se = ("ve", inc("ve", lambda v: v.tensor_copy(
                sa[:, 4 * T:8 * T], xt[:, 4 * T:8 * T]), "ve"))
            drain("ve")
        else:
            cast_se = ("se", inc("se", lambda s: s.copy(sa[:, 4 * T:8 * T], xt[:, 4 * T:8 * T]), "se"))
        yield
        wait("pe", "ve", cast_ve)
        for c in range(NCH):
            if c == 4:
                wait_tok("pe", cast_se)
            fn = lambda tt, _c=c: tt.matmul(B3[0:1, 0:T], onescol[:, :],
                                            sa[:, _c * T:(_c + 1) * T],
                                            start=(_c == 0), stop=(_c == NCH - 1))
            if c == NCH - 1:
                inc("pe", fn, "pe")
            else:
                op("pe", fn)
        yield
        mm_sq = {}
        sq_done = {}

        def emit_sq(cp):
            pp = cp % 2
            if cp == 0:
                wait("se", "ve", cast_ve)
            if cp == 2:
                wait_tok("se", cast_se)
            if cp >= 2:
                wait("se", "pe", mm_sq.get(2 * cp - 3, 0))
            vv = sa[:, 2 * cp * T:(2 * cp + 2) * T]
            sq_done[cp] = inc("se", lambda s, _vv=vv, _p=pp: s.activation(
                yec[:, _p * 2 * T:(_p + 1) * 2 * T], _vv, ACT.Square), "se")
            drain("se")

        def emit_mmsq(c):
            cp = c // 2
            slot = (cp % 2) * 2 + (c % 2)
            wait("pe", "se", sq_done[cp])
            mm_sq[c] = inc("pe", lambda tt, _c=c, _s=slot: tt.matmul(
                B3[32:33, 0:T], onescol[:, :], yec[:, _s * T:(_s + 1) * T],
                start=(_c == 0), stop=(_c == NCH - 1)), "pe")

        emit_sq(0)
        emit_sq(1)
        yield
        for c in range(NCH):
            emit_mmsq(c)
            if c % 2 == 1 and c // 2 + 2 < NCH // 2:
                emit_sq(c // 2 + 2)
            if c % 2 == 1:
                yield
        sumsq_done = mm_sq[NCH - 1]
        wait("ve", "pe", sumsq_done)
        op("ve", lambda v: v.tensor_scalar(St.Msb[:, :], B3[0:1, 0:T], INV_D, None, ALU.mult))
        op("ve", lambda v: v.tensor_scalar(B3[64:65, 0:T], B3[32:33, 0:T], INV_D, EPS, ALU.mult, ALU.add))
        drain("ve")
        op("ve", lambda v: v.tensor_mul(St.MRb[:, :], St.Msb[:, :], St.Msb[:, :]))
        drain("ve")
        statv_done = inc("ve", lambda v: v.tensor_sub(B3[64:65, 0:T], B3[64:65, 0:T], St.MRb[:, :]), "ve")
        drain("ve")
        mb_done = inc("ve", lambda v: v.tensor_copy(St.MRb[:, :], St.Msb[:, :]), "ve")
        yield
        wait("se", "ve", statv_done)
        op("se", lambda s: s.activation(B3[96:97, 0:T], B3[64:65, 0:T], ACT.Ln))
        drain("se")
        op("se", lambda s: s.activation(B3[64:65, 0:T], B3[96:97, 0:T], ACT.Exp, scale=-0.5))
        drain("se")
        r_done = inc("se", lambda s: s.copy(St.Rb[:, :], B3[64:65, 0:T]), "se")
        yield
        wait("pe", "se", r_done)
        wait("pe", "ve", mb_done)
        op("pe", lambda tt: tt.matmul(B0[:, 0:T], onesrow[:, :], St.Rb[:, :], start=True, stop=True))
        bc_done = inc("pe", lambda tt: tt.matmul(B1[:, 0:T], onesrow[:, :], St.MRb[:, :], start=True, stop=True), "pe")
        yield
        wait("se", "pe", bc_done)
        op("se", lambda s: s.copy(St.rbb[:, :], B0[:, 0:T]))
        drain("se")
        bcv_done = inc("se", lambda s: s.copy(St.mrbb[:, :], B1[:, 0:T]), "se")
        drain("se")
        wait("ve", "se", bcv_done)
        y_pair = {}
        for cp in range(NCH // 2):
            pp = cp % 2
            sv = sa[:, 2 * cp * T:(2 * cp + 2) * T].rearrange("p (h n) -> p h n", h=2)
            yv = yec[:, pp * 2 * T:(pp + 1) * 2 * T].rearrange("p (h n) -> p h n", h=2)
            op("ve", lambda v, _s=sv, _y=yv: v.tensor_sub(
                _y, _s, St.mrbb[:, :].unsqueeze(1).to_broadcast([128, 2, T])))
            drain("ve")
            y_pair[cp] = inc("ve", lambda v, _s=sv, _y=yv: v.tensor_mul(
                _s, _y, St.rbb[:, :].unsqueeze(1).to_broadcast([128, 2, T])), "ve")
            drain("ve")
            if cp % 2 == 1:
                yield
        St.st["y_pair"] = y_pair
        St.st["y_done"] = y_pair[NCH // 2 - 1]

    def gate_chain(St):
        sa, B2 = St.SA, St.B[2]
        y_done = St.st["y_done"]
        y_pair = St.st["y_pair"]
        logits_done = None
        for c in range(NCH):
            if c % 2 == 0:
                wait("pe", "ve", y_pair[c // 2])
            fn = lambda tt, _c=c: tt.matmul(
                B2[0:8, 0:T], Wg[:, _c * NE:(_c + 1) * NE],
                sa[:, _c * T:(_c + 1) * T],
                start=(_c == 0), stop=(_c == NCH - 1))
            if c == NCH - 1:
                logits_done = inc("pe", fn, "pe")
            else:
                op("pe", fn)
        yield
        wait("se", "pe", logits_done)
        cf_ev = inc("se", lambda s: s.copy(St.Cf[0:8, :], B2[0:8, 0:T]), "se")
        drain("se")
        yield
        wait("pe", "se", cf_ev)
        trl_done = None
        for g in range(NG):
            fn = lambda tt, _g=g: tt.transpose(
                St.B2b[0:128, _g * 8:(_g + 1) * 8],
                St.Cf[0:8, _g * 128:(_g + 1) * 128], id128[0:8, 0:8])
            if g == NG - 1:
                trl_done = inc("pe", fn, "pe")
            else:
                op("pe", fn)
        yield
        wait("se", "pe", trl_done)
        lg_done = inc("se", lambda s: s.copy(St.Lg[:, :], St.B2b[:, 0:NG * 8]), "se")
        drain("se")
        wait("ve", "se", lg_done)
        for g in range(NG):
            op("ve", lambda v, _g=g: v.max(out=St.M8[:, _g * 8:(_g + 1) * 8],
                                           in_=St.Lg[:, _g * 8:(_g + 1) * 8]))
        drain("ve")
        m8r = St.M8[:, 0:NG * 8].rearrange("p (g e) -> p g e", g=NG)
        lgr = St.Lg[:, :].rearrange("p (g e) -> p g e", g=NG)
        op("ve", lambda v: v.tensor_sub(
            St.ELg[:, 16:16 + NG], m8r[:, :, 1:2], m8r[:, :, 0:1]))
        op("ve", lambda v: v.tensor_sub(St.ELg[:, 0:NG * 8].rearrange("p (g e) -> p g e", g=NG),
                                        lgr, m8r[:, :, 0:1].to_broadcast([128, NG, 8])))
        dl_done = inc("ve", lambda v: v.tensor_tensor(
            St.SELg[:, :].rearrange("p (g e) -> p g e", g=NG), lgr,
            m8r[:, :, 1:2].to_broadcast([128, NG, 8]), ALU.is_ge), "ve")
        yield
        wait("se", "ve", dl_done)
        el_done = inc("se", lambda s: s.activation(St.M8[:, :], St.ELg[:, :], ACT.Exp), "se")
        yield
        wait("ve", "se", el_done)
        op("ve", lambda v: v.tensor_scalar(St.D2[:, :], St.M8[:, 16:16 + NG], 1.0, 2.0, ALU.add, ALU.mult))
        drain("ve")
        op("ve", lambda v: v.reciprocal(St.RR[:, :], St.D2[:, :]))
        drain("ve")
        op("ve", lambda v: v.tensor_tensor(
            St.ELg[:, 0:NG * 8].rearrange("p (g e) -> p g e", g=NG),
            St.M8[:, 0:NG * 8].rearrange("p (g e) -> p g e", g=NG),
            St.RR[:, :].unsqueeze(2).to_broadcast([128, NG, 8]), ALU.mult))
        drain("ve")
        wtm_done = inc("ve", lambda v: v.tensor_mul(
            St.wtm[:, :], St.ELg[:, 0:NG * 8], St.SELg[:, :]), "ve")
        yield
        wait("pe", "ve", wtm_done)
        tr_done = None
        for g in range(NG):
            fn = lambda tt, _g=g: tt.transpose(
                St.B3b[0:8, _g * 128:(_g + 1) * 128], St.wtm[:, _g * 8:(_g + 1) * 8], id128[:, :])
            if g == NG - 1:
                tr_done = inc("pe", fn, "pe")
            else:
                op("pe", fn)
        yield
        wait("ve", "pe", tr_done)
        wem_done = inc("ve", lambda v: v.tensor_copy(St.w_em[:, :], St.B3b[0:8, 0:NG * 128]), "ve")
        yield
        wait("pe", "ve", wem_done)
        wbc_evac = {}
        for e in range(NE):
            bank = St.B[e % 4]
            if e >= 4:
                wait("pe", "se", wbc_evac[e - 4])
            mmv = inc("pe", lambda tt, _b=bank, _e=e: tt.matmul(
                _b[:, 0:T], oneh[0:8, _e * 128:(_e + 1) * 128], St.w_em[0:8, :],
                start=True, stop=True), "pe")
            wait("se", "pe", mmv)
            wbc_evac[e] = inc("se", lambda s, _b=bank, _e=e: s.copy(
                St.SC[:, _e * T:(_e + 1) * T], _b[:, 0:T]), "se")
            drain("se")
            if e % 2 == 1:
                yield
        St.st["wbc_done"] = ("se", wbc_evac[NE - 1])
        St.st["wbc_e"] = {e: ("se", wbc_evac[e]) for e in range(NE)}


    def ffn_phase(St, t, first_ffn=False):
        xt = XT[t % NXB]
        sa, sc, yec = St.SA, St.SC, St.yec
        wait_tok("pe", St.st["wbc_done"])
        mm_done = {}
        i = 0
        for e in range(NE):
            if first_ffn:
                wait_tok("pe", exp_loaded[e])
            for cp in range(NCH // 2):
                c0 = 2 * cp
                pp = i % 4
                if cp == 0:
                    wait_tok("gp", St.st["wbc_e"][e])
                if i >= 4:
                    wait("gp", "pe", mm_done[i - 4])
                yd = inc("gp", lambda g, _c=c0, _e=e, _p=pp: g.tensor_mul(
                    yec[:, _p * 2 * T:(_p + 1) * 2 * T].rearrange("p (h n) -> p h n", h=2),
                    sa[:, _c * T:(_c + 2) * T].rearrange("p (h n) -> p h n", h=2),
                    sc[:, _e * T:(_e + 1) * T].unsqueeze(1).to_broadcast([128, 2, T])), "gq")
                wait("pe", "gq", yd)
                for half in range(2):
                    c = c0 + half
                    for dj in range(8):
                        bank = St.B[dj % 4]
                        lo = (dj // 4) * 256
                        fn = lambda tt, _e=e, _c=c, _dj=dj, _p=pp, _h=half, _bk=bank, _lo=lo: tt.matmul(
                            _bk[:, _lo:_lo + T],
                            Wexp[:, ((_e * NCH + _c) * D + _dj * 128):((_e * NCH + _c) * D + (_dj + 1) * 128)],
                            yec[:, (_p * 2 + _h) * T:(_p * 2 + _h + 1) * T],
                            start=(_e == 0 and _c == 0 and _dj < 4),
                            stop=(_e == NE - 1 and _c == NCH - 1 and _dj >= 4))
                        if half == 1 and dj == 7:
                            mm_done[i] = inc("pe", fn, "pe")
                        else:
                            op("pe", fn)
                i += 1
                yield
        ffn_mm_done = mm_done[31]
        wait("ve", "pe", ffn_mm_done)
        evac = None
        for bkid in range(4):
            bank = St.B[bkid]
            xv = xt[:, :].rearrange("p (h c n) -> p h c n", h=2, c=4)[:, :, bkid, :]
            evac = inc("ve", lambda v, _bk=bank, _xv=xv: v.tensor_tensor(
                _xv, _xv, _bk[:, :].rearrange("p (h n) -> p h n", h=2), ALU.add), "ve")
        St.st["ffn_evac"] = evac

    def attn_phase(St, t):
        xt = XT[t % NXB]
        sa, sc, yec = St.SA, St.SC, St.yec
        B0, B1, B2 = St.B[0], St.B[1], St.B[2]
        y_done = St.st["y_done"]
        blk_free = {}

        def lhs(i, c, dj):
            buf = i % NWB
            base = buf * 2048 + c * 256 + (dj % 2) * 128
            return Wab[:, base:base + 128]

        order = []
        for j2 in range(4):
            order += [0 * 4 + j2, 1 * 4 + j2]
        order += [2 * 4 + j2 for j2 in range(4)]
        order += [3 * 4 + j2 for j2 in range(4)]
        pos = {b: idx for idx, b in enumerate(order)}

        def load_for(idx):
            if idx >= 16:
                return
            b = order[idx]
            if idx - NWB >= 0:
                tok = ("pe", blk_free[order[idx - NWB]])
            else:
                tok = attn_pool_free.get(idx % NWB)
            emit_blk_load(t, b, idx, tok)

        for idx in range(min(NWB, 16)):
            load_for(idx)

        wait("pe", "ve", y_done)
        qk_done = {}
        se_prev = {}
        se_q = {}
        q_read = {}
        k_read = {}
        for j2 in range(4):
            qb, kb = 0 * 4 + j2, 1 * 4 + j2
            wait_tok("pe", blk_loaded[(t, qb)])
            wait_tok("pe", blk_loaded[(t, kb)])
            for dj in (2 * j2, 2 * j2 + 1):
                if dj >= 1:
                    wait("pe", "se", q_read[dj - 1])
                for c in range(NCH):
                    fn = lambda tt, _c=c, _dj=dj, _i=pos[qb]: tt.matmul(
                        B0[:, 0:T], lhs(_i, _c, _dj), sa[:, _c * T:(_c + 1) * T],
                        start=(_c == 0), stop=(_c == NCH - 1))
                    if c == NCH - 1:
                        qmm = inc("pe", fn, "pe")
                    else:
                        op("pe", fn)
                if dj >= 1:
                    wait("pe", "ve", k_read[dj - 1])
                for c in range(NCH):
                    fn = lambda tt, _c=c, _dj=dj, _i=pos[kb]: tt.matmul(
                        B1[:, 0:T], lhs(_i, _c, _dj), sa[:, _c * T:(_c + 1) * T],
                        start=(_c == 0), stop=(_c == NCH - 1))
                    if c == NCH - 1:
                        kmm = inc("pe", fn, "pe")
                    else:
                        op("pe", fn)
                wait("se", "pe", qmm)
                if dj >= 1:
                    wait("se", "ve", qk_done[dj - 1])
                q_read[dj] = inc("se", lambda s, _dj=dj: s.activation(
                    yec[:, 0:T], B0[:, 0:T], ACT.Square,
                    bias=Bias[:, 0 * 8 + _dj:0 * 8 + _dj + 1]), "se")
                drain("se")
                op("se", lambda s: s.activation(yec[:, T:2 * T], yec[:, 0:T], ACT.Exp, scale=-0.5))
                drain("se")
                wait("ve", "pe", kmm)
                if dj >= 1:
                    wait("ve", "se", se_prev[dj - 1])
                k_read[dj] = inc("ve", lambda v, _dj=dj: v.tensor_scalar(
                    yec[:, 2 * T:3 * T], B1[:, 0:T], Bias[:, 1 * 8 + _dj:1 * 8 + _dj + 1], None, ALU.add), "ve")
                drain("ve")
                sq_done = inc("ve", lambda v: v.tensor_mul(
                    yec[:, 2 * T:3 * T], yec[:, 2 * T:3 * T], yec[:, 2 * T:3 * T]), "ve")
                wait("se", "ve", sq_done)
                se_prev[dj] = inc("se", lambda s: s.activation(
                    yec[:, 3 * T:4 * T], yec[:, 2 * T:3 * T], ACT.Exp, scale=-0.5), "se")
                drain("se")
                wait("ve", "se", se_prev[dj])
                qk_done[dj] = inc("ve", lambda v, _dj=dj: v.tensor_mul(
                    sc[:, _dj * T:(_dj + 1) * T], yec[:, T:2 * T], yec[:, 3 * T:4 * T]), "ve")
                yield
            blk_free[qb] = C["pe"]
            blk_free[kb] = C["pe"]
            load_for(pos[qb] + NWB)
            load_for(pos[kb] + NWB)
        coeff_done = None
        for c in range(NCH):
            wait("pe", "ve", qk_done[c])
            fn = lambda tt, _c=c: tt.matmul(
                B2[0:16, 0:T], IND[:, _c * 16:(_c + 1) * 16], sc[:, _c * T:(_c + 1) * T],
                start=(_c == 0), stop=(_c == NCH - 1))
            if c == NCH - 1:
                coeff_done = inc("pe", fn, "pe")
            else:
                op("pe", fn)
        wait("ve", "pe", coeff_done)
        cf_done = inc("ve", lambda v: v.tensor_copy(St.Cf[:, :], B2[0:16, 0:T]), "ve")
        yield
        vevac = {}
        for dj in range(8):
            b = 2 * 4 + dj // 2
            bank = B0 if dj % 2 == 0 else B1
            if dj % 2 == 0:
                wait_tok("pe", blk_loaded[(t, b)])
            if dj >= 2:
                wait("pe", "ve", vevac[dj - 2])
            for c in range(NCH):
                fn = lambda tt, _c=c, _dj=dj, _i=pos[b], _bk=bank: tt.matmul(
                    _bk[:, 0:T], lhs(_i, _c, _dj), sa[:, _c * T:(_c + 1) * T],
                    start=(_c == 0), stop=(_c == NCH - 1))
                if c == NCH - 1:
                    vmm = inc("pe", fn, "pe")
                else:
                    op("pe", fn)
            if dj % 2 == 1:
                blk_free[b] = C["pe"]
                load_for(pos[b] + NWB)
            cbank = St.B[2 + dj % 2]
            if dj == 0:
                wait("pe", "ve", cf_done)
            if dj >= 2:
                wait("pe", "ve", vevac[dj - 2])
            expd = inc("pe", lambda tt, _dj=dj, _cb=cbank: tt.matmul(
                _cb[:, 0:T], EXPD[0:16, _dj * 128:(_dj + 1) * 128], St.Cf[0:16, :],
                start=True, stop=True), "pe")
            wait("ve", "pe", vmm)
            wait("ve", "pe", expd)
            op("ve", lambda v, _dj=dj, _bk=bank: v.tensor_scalar(
                yec[:, 0:T], _bk[:, 0:T],
                Bias[:, 2 * 8 + _dj:2 * 8 + _dj + 1], None, ALU.add))
            drain("ve")
            vevac[dj] = inc("ve", lambda v, _dj=dj, _cb=cbank: v.scalar_tensor_tensor(
                sc[:, _dj * T:(_dj + 1) * T], _cb[:, 0:T], 1.0,
                yec[:, 0:T], ALU.bypass, ALU.mult), "ve")
            drain("ve")
            yield
        St.st["attn_sa_free_pe"] = C["pe"]
        oevac = {}
        wait("pe", "ve", vevac[7])
        for dj in range(8):
            b = 3 * 4 + dj // 2
            bank = B0 if dj % 2 == 0 else B1
            if dj % 2 == 0:
                wait_tok("pe", blk_loaded[(t, b)])
            if dj >= 2:
                wait("pe", "ve", oevac[dj - 2])
            for c in range(NCH):
                fn = lambda tt, _c=c, _dj=dj, _i=pos[b], _bk=bank: tt.matmul(
                    _bk[:, 0:T], lhs(_i, _c, _dj), sc[:, _c * T:(_c + 1) * T],
                    start=(_c == 0), stop=(_c == NCH - 1))
                if c == NCH - 1:
                    omm = inc("pe", fn, "pe")
                else:
                    op("pe", fn)
            if dj % 2 == 1:
                blk_free[b] = C["pe"]
            wait("ve", "pe", omm)
            oevac[dj] = inc("ve", lambda v, _dj=dj, _bk=bank: v.scalar_tensor_tensor(
                xt[:, _dj * T:(_dj + 1) * T], _bk[:, 0:T],
                Bias[:, 3 * 8 + _dj:3 * 8 + _dj + 1], xt[:, _dj * T:(_dj + 1) * T],
                ALU.add, ALU.add), "ve")
            yield
        St.st["attn_evac"] = oevac[7]
        for k in range(4):
            attn_pool_free[(12 + k) % NWB] = ("pe", blk_free[order[12 + k]])


    def run(heavy, chain, delay=2, stride=1):
        k = 0
        for _ in heavy:
            k += 1
            if (chain is not None and k >= delay
                    and (k - delay) % stride == 0):
                try:
                    next(chain)
                except StopIteration:
                    chain = None
        while chain is not None:
            try:
                next(chain)
            except StopIteration:
                chain = None

    def seq(*gens):
        for g in gens:
            for x in g:
                yield x

    def full(gen):
        for _ in gen:
            pass

    def store_tile(St, t):
        buf = t % NXB
        sname = "t%d" % buf
        wait("sy", sname, C[sname])
        wait("sy", "ve", St.st["ffn_evac"])
        x_stored[t] = (sname, inc("sy", lambda s, _t=t, _b=buf: s.dma_start(
            x_out[_t, :, :, :].rearrange("p c n -> p (c n)"), XT[_b][:, :]), sname, 16))
        St.st["tile_evac"] = St.st["ffn_evac"]
        emit_x_load(t + NXB)

    for tok in consts_loaded:
        wait_tok("pe", tok)
    wait_tok("ve", bias_tok)
    wait_tok("se", bias_tok)

    npair = nt_count // 2
    full(seq(ln_chain(S[0], 0, 1), gate_chain(S[0])))
    for p in range(npair):
        a, b = 2 * p, 2 * p + 1
        A, Bs = S[0], S[1]
        run(ffn_phase(A, a, first_ffn=(p == 0)), seq(ln_chain(Bs, b, 1), gate_chain(Bs)))
        run(ffn_phase(Bs, b, first_ffn=(p == 0)), ln_chain(A, a, 2), delay=4, stride=2)
        run(attn_phase(A, a), ln_chain(Bs, b, 2, use_gp=True), delay=4, stride=2)
        run(attn_phase(Bs, b), seq(ln_chain(A, a, 3, use_gp=True), gate_chain(A)), delay=2, stride=1)
        run(ffn_phase(A, a), seq(ln_chain(Bs, b, 3), gate_chain(Bs)))
        store_tile(A, a)
        nxt = seq(ln_chain(A, 2 * p + 2, 1), gate_chain(A)) if p + 1 < npair else None
        run(ffn_phase(Bs, b), nxt)
        store_tile(Bs, b)

    for i in range(NXB):
        wait("sy", "t%d" % i, C["t%d" % i])

    blk = ctx.enter_context(nc.Block())

    @blk.tensor
    def _(t):
        for fn in Q["pe"]:
            fn(t)

    @blk.vector
    def _(v):
        for fn in Q["ve"]:
            fn(v)

    @blk.scalar
    def _(s):
        for fn in Q["se"]:
            fn(s)

    @blk.sync
    def _(s):
        for fn in Q["sy"]:
            fn(s)

    @blk.gpsimd
    def _(g):
        for fn in Q["gp"]:
            fn(g)

    ctx.close()
    return nc


def _pack_inputs(L0_attn_x, L0_attn_Wq, L0_attn_bq, L0_attn_Wk, L0_attn_bk,
                 L0_attn_Wv, L0_attn_bv, L0_attn_Wo, L0_attn_bo,
                 L0_ffn_gate_w, L0_ffn_expert_w):
    x = np.ascontiguousarray(np.asarray(L0_attn_x, dtype=np.float32)).reshape(-1, D)
    n_tok = x.shape[0]
    per = n_tok // NCORES

    ew = np.ascontiguousarray(np.asarray(L0_ffn_expert_w, dtype=np.float32))
    ewT = ew.transpose(0, 2, 1)
    w_exp = np.ascontiguousarray(
        ewT.reshape(NE, NCH, 128, D).transpose(0, 2, 1, 3).reshape(NE, 128, NCH * D))

    def attn_pack(W):
        W = np.asarray(W, dtype=np.float32)
        return W.reshape(NCH, 128, 4, 256).transpose(2, 1, 0, 3).reshape(4, 128, NCH * 256)
    w_attn = np.ascontiguousarray(np.stack(
        [attn_pack(L0_attn_Wq), attn_pack(L0_attn_Wk),
         attn_pack(L0_attn_Wv), attn_pack(L0_attn_Wo)]))

    bs = np.stack([np.asarray(bv, dtype=np.float32) for bv in
                   (L0_attn_bq, L0_attn_bk, L0_attn_bv, L0_attn_bo)])
    biases = np.ascontiguousarray(bs.reshape(4, NCH, 128).transpose(2, 0, 1).reshape(128, 32))

    gw = np.asarray(L0_ffn_gate_w, dtype=np.float32)
    w_gate = np.ascontiguousarray(
        gw.T.reshape(NCH, 128, NE).transpose(1, 0, 2).reshape(128, NCH * NE))

    c_onescol = np.ones((128, 1), np.float32)
    c_onesrow = np.ones((1, 128), np.float32)
    c_id = np.eye(128, dtype=np.float32)
    c_oneh = np.zeros((8, 1024), np.float32)
    for e in range(8):
        c_oneh[e, e * 128:(e + 1) * 128] = 1.0
    c_ind = np.zeros((128, 128), np.float32)
    for c in range(NCH):
        for pp in range(128):
            c_ind[pp, c * 16 + 2 * c + pp // 64] = 1.0
    c_expd = np.zeros((16, 1024), np.float32)
    for dj in range(NCH):
        for pp in range(128):
            c_expd[2 * dj + pp // 64, dj * 128 + pp] = 1.0

    shared = {
        "w_exp": w_exp, "w_attn": w_attn, "w_gate": w_gate, "biases": biases,
        "c_onescol": c_onescol, "c_onesrow": c_onesrow, "c_id": c_id,
        "c_oneh": c_oneh, "c_ind": c_ind, "c_expd": c_expd,
    }
    in_maps = []
    for ci in range(NCORES):
        xc = x[ci * per:(ci + 1) * per]
        x_in = np.ascontiguousarray(
            xc.reshape(NT, T, NCH, 128).transpose(0, 3, 2, 1))
        m = dict(shared)
        m["x_in"] = x_in
        in_maps.append(m)
    return in_maps


def kernel(L0_attn_x, L0_attn_Wq, L0_attn_bq, L0_attn_Wk, L0_attn_bk,
           L0_attn_Wv, L0_attn_bv, L0_attn_Wo, L0_attn_bo,
           L0_ffn_gate_w, L0_ffn_expert_w):
    in_maps = _pack_inputs(L0_attn_x, L0_attn_Wq, L0_attn_bq, L0_attn_Wk,
                           L0_attn_bk, L0_attn_Wv, L0_attn_bv, L0_attn_Wo,
                           L0_attn_bo, L0_ffn_gate_w, L0_ffn_expert_w)
    nc = build_program()
    res = run_bass_kernel_spmd(nc, in_maps, core_ids=list(range(NCORES)))
    outs = []
    for ci in range(NCORES):
        o = np.asarray(res.results[ci]["x_out"])
        outs.append(o.transpose(0, 3, 2, 1).reshape(NT * T, D))
    full_out = np.concatenate(outs, axis=0)
    Bb, Sq = np.asarray(L0_attn_x).shape[0], np.asarray(L0_attn_x).shape[1]
    return np.ascontiguousarray(full_out.reshape(Bb, Sq, D).astype(np.float32))
